# version 10
# baseline (speedup 1.0000x reference)
import numpy as np
import ml_dtypes
from contextlib import ExitStack

import concourse.bass as bass
import concourse.mybir as mybir
from concourse.bass_utils import run_bass_kernel_spmd

F32 = mybir.dt.float32
BF16 = mybir.dt.bfloat16
I32 = mybir.dt.int32
AL = mybir.AluOpType
ACT = mybir.ActivationFunctionType
AX = mybir.AxisListType
BF = ml_dtypes.bfloat16
ts = bass.ts

NC_CORES = 8
B_, T_, D = 4, 4096, 1024
DI = 2048
S = 16
FF = 4096
T_OWN = 2048
TC = 512
NTC = T_OWN // TC
G = DI // 128
DCH = D // 128
NCH_P = (2 * DI) // 128
FCH = FF // 128
EPS = 1e-5
M0, M1 = 0x5EC0FFEE, 0x1337BEEF


def build_program():
    nc = bass.Bass("TRN2", target_bir_lowering=False, debug=False,
                   detect_race_conditions=bool(globals().get("DBG_RACE")))

    xin = nc.dram_tensor("xin", [T_OWN + 3, D], F32, kind="ExternalInput").ap()
    prm = nc.dram_tensor("prm", [1, 8], I32, kind="ExternalInput").ap()
    maskc = nc.dram_tensor("maskc", [128, 1], F32, kind="ExternalInput").ap()
    identb = nc.dram_tensor("identb", [128, 128], BF16, kind="ExternalInput").ap()
    identf = nc.dram_tensor("identf", [128, 128], F32, kind="ExternalInput").ap()
    onesb = nc.dram_tensor("onesb", [1, 128], BF16, kind="ExternalInput").ap()
    onesf_in = nc.dram_tensor("onesf_in", [1, 128], F32, kind="ExternalInput").ap()
    selb_in = nc.dram_tensor("selb_in", [16, 16 * 128], BF16, kind="ExternalInput").ap()

    w_in = nc.dram_tensor("w_in", [D, 2 * DI], F32, kind="ExternalInput").ap()
    b_in = nc.dram_tensor("b_in", [2 * DI], F32, kind="ExternalInput").ap()
    ln_g = nc.dram_tensor("ln_g", [D], F32, kind="ExternalInput").ap()
    ln_b = nc.dram_tensor("ln_b", [D], F32, kind="ExternalInput").ap()
    conv_w = nc.dram_tensor("conv_w", [DI, 4], F32, kind="ExternalInput").ap()
    conv_b = nc.dram_tensor("conv_b", [DI], F32, kind="ExternalInput").ap()
    amat_in = nc.dram_tensor("amat_in", [DI, S], F32, kind="ExternalInput").ap()
    w_bm = nc.dram_tensor("w_bm", [DI, S], F32, kind="ExternalInput").ap()
    b_bm = nc.dram_tensor("b_bm", [S], F32, kind="ExternalInput").ap()
    w_cm = nc.dram_tensor("w_cm", [DI, S], F32, kind="ExternalInput").ap()
    b_cm = nc.dram_tensor("b_cm", [S], F32, kind="ExternalInput").ap()
    dvec = nc.dram_tensor("dvec", [DI], F32, kind="ExternalInput").ap()
    w_out = nc.dram_tensor("w_out", [DI, D], F32, kind="ExternalInput").ap()
    b_out = nc.dram_tensor("b_out", [D], F32, kind="ExternalInput").ap()

    w_fc = nc.dram_tensor("w_fc", [FF, D], F32, kind="ExternalInput").ap()
    b_fc = nc.dram_tensor("b_fc", [FF], F32, kind="ExternalInput").ap()
    a_fc = nc.dram_tensor("a_fc", [16, D], F32, kind="ExternalInput").ap()
    bl_fc = nc.dram_tensor("bl_fc", [FF, 16], F32, kind="ExternalInput").ap()
    w_pj = nc.dram_tensor("w_pj", [D, FF], F32, kind="ExternalInput").ap()
    b_pj = nc.dram_tensor("b_pj", [D], F32, kind="ExternalInput").ap()
    a_pj = nc.dram_tensor("a_pj", [16, FF], F32, kind="ExternalInput").ap()
    bl_pj = nc.dram_tensor("bl_pj", [D, 16], F32, kind="ExternalInput").ap()

    out = nc.dram_tensor("out", [T_OWN, D], F32, kind="ExternalOutput").ap()

    _scr = "ExternalOutput" if globals().get("DBG_SCRATCH") else "Internal"
    dbg_dram = nc.dram_tensor("dbg_scr", [128, 8], F32, kind=_scr).ap()
    xc_dram = nc.dram_tensor("xc_scr", [DI, T_OWN], BF16, kind=_scr).ap()
    sg_dram = nc.dram_tensor("sg_scr", [DI, T_OWN], BF16, kind=_scr).ap()
    r1_dram = nc.dram_tensor("r1_scr", [T_OWN, D], F32, kind=_scr).ap()
    _dbg = bool(globals().get("DBG_SCRATCH"))
    if _dbg:
        x1n_dram = nc.dram_tensor("x1n_scr", [T_OWN, D], F32, kind=_scr).ap()
        x2_dram = nc.dram_tensor("x2_scr", [T_OWN, D], F32, kind=_scr).ap()
        zab_dram = nc.dram_tensor("zab_scr", [NTC * 32, TC], BF16,
                                  kind=_scr).ap()
        yb_dram = nc.dram_tensor("ybuf_scr", [NTC * 128, G * TC], BF16,
                                 kind=_scr).ap()
        st_dram = nc.dram_tensor("st_scr", [NTC * 128, G * S], F32,
                                 kind=_scr).ap()
        at_dram = nc.dram_tensor("attn_scr", [NTC * DCH * 128, TC], F32,
                                 kind=_scr).ap()
        xa_dram = nc.dram_tensor("xa_scr", [NTC * 128, 4 * D], F32,
                                 kind=_scr).ap()
        wo_dram = nc.dram_tensor("woutb_scr", [128, G * D], BF16,
                                 kind=_scr).ap()
        bo_dram = nc.dram_tensor("boutc_scr", [128, DCH], F32,
                                 kind=_scr).ap()
    wqfc_dram = nc.dram_tensor("wqfc_scr", [FCH * DCH * 128, 128], BF16, kind=_scr).ap()
    wqpj_dram = nc.dram_tensor("wqpj_scr", [DCH * FCH * 128, 128], BF16, kind=_scr).ap()

    shs = nc.dram_tensor("xch_states", [NC_CORES * 128, G * S], F32,
                         kind="Internal", addr_space="Shared").ap()
    shf = nc.dram_tensor("xch_flags", [NC_CORES, 2], I32,
                         kind="Internal", addr_space="Shared").ap()

    with ExitStack() as st_:
        E = st_.enter_context
        prms = E(nc.sbuf_tensor("prms", [1, 8], I32))
        mask = E(nc.sbuf_tensor("mask", [128, 1], F32))
        idb = E(nc.sbuf_tensor("idb", [128, 128], BF16))
        idf = E(nc.sbuf_tensor("idf", [128, 128], F32))
        ones1 = E(nc.sbuf_tensor("ones1", [1, 128], BF16))
        onesf = E(nc.sbuf_tensor("onesf", [1, 128], F32))
        selb = E(nc.sbuf_tensor("selb", [16, 16 * 128], BF16))
        lngc = E(nc.sbuf_tensor("lngc", [128, DCH], F32))
        lnbc = E(nc.sbuf_tensor("lnbc", [128, DCH], F32))
        binc = E(nc.sbuf_tensor("binc", [128, NCH_P], F32))
        c2c = E(nc.sbuf_tensor("c2c", [128, NCH_P], F32))
        amat = E(nc.sbuf_tensor("amat", [128, G * S], F32))
        cwc = E(nc.sbuf_tensor("cwc", [128, G * 4], F32))
        cbc = E(nc.sbuf_tensor("cbc", [128, G], F32))
        dcol = E(nc.sbuf_tensor("dcol", [128, G], F32))
        bBc = E(nc.sbuf_tensor("bBc", [16, 1], F32))
        bCc = E(nc.sbuf_tensor("bCc", [16, 1], F32))
        boutc = E(nc.sbuf_tensor("boutc", [128, DCH], F32))
        bfcc = E(nc.sbuf_tensor("bfcc", [128, FCH], F32))
        bpjc = E(nc.sbuf_tensor("bpjc", [128, DCH], F32))
        wB = E(nc.sbuf_tensor("wB", [128, G * S], BF16))
        wC = E(nc.sbuf_tensor("wC", [128, G * S], BF16))
        afcT = E(nc.sbuf_tensor("afcT", [128, DCH * 16], BF16))
        apjT = E(nc.sbuf_tensor("apjT", [128, FCH * 16], BF16))
        bpjT = E(nc.sbuf_tensor("bpjT", [16, D], BF16))
        scl = E(nc.sbuf_tensor("scl", [128, 4], F32))
        scl16 = E(nc.sbuf_tensor("scl16", [16, 2], F32))
        redc = E(nc.sbuf_tensor("redc", [128, 34], F32))
        red1 = E(nc.sbuf_tensor("red1", [1, 4], F32))

        ar2_t = E(nc.sbuf_tensor("ar2", [128, 57344], BF16))
        wstg = ar2_t[:, 36864:37888]
        wtb = ar2_t[:, 0:32768]
        xnfm = ar2_t[:, 36864:36864 + DCH * TC]
        b128 = ar2_t[:, 0:S * TC]
        c128 = ar2_t[:, S * TC:2 * S * TC]
        ubuf = ar2_t[:, 2 * S * TC:3 * S * TC]
        hbuf = ar2_t[:, 3 * S * TC:4 * S * TC]
        ybuf = ar2_t[:, 4 * S * TC:4 * S * TC + G * TC]
        woutb = ar2_t[:, 40960:40960 + G * D]
        h1fm = ar2_t[:, 0:FCH * TC]
        x1fm = ar2_t[:, 36864:36864 + DCH * TC]
        bfcT = ar2_t[0:16, 40960:40960 + FF]
        ar3_t = E(nc.sbuf_tensor("ar3", [128, 16384], BF16))
        xcch = ar3_t[:, 0:G * TC]
        sgch = ar3_t[:, G * TC:2 * G * TC]
        wslab_t = E(nc.sbuf_tensor("wslab", [128, 4096], F32))
        wslab = wslab_t[:, :]
        x1fs = wslab_t[:, :]
        ki32 = E(nc.sbuf_tensor("ki32", [128, 512], I32))
        kfq = E(nc.sbuf_tensor("kfq", [128, 512], F32))
        fcwin_t = E(nc.sbuf_tensor("fcwin", [128, 2 * DCH * 128], BF16))
        pjwin_t = E(nc.sbuf_tensor("pjwin", [128, 2048], BF16))


        xtok = E(nc.sbuf_tensor("xtok", [128, D], F32))
        bnst = E(nc.sbuf_tensor("bnst", [128, 12], F32))
        bnag = E(nc.sbuf_tensor("bnag", [128, 4], F32))
        rrc = E(nc.sbuf_tensor("rrc", [128, 1], F32))
        epsc = E(nc.sbuf_tensor("epsc", [128, 1], F32))
        nmr = E(nc.sbuf_tensor("nmr", [128, 1], F32))
        xntok = ar2_t[:, 35840:36864]
        xnh = ar2_t[0:3, 34816:35840]
        xnhfm = E(nc.sbuf_tensor("xnhfm", [128, DCH * 3], BF16))
        xhsc = E(nc.sbuf_tensor("xhsc", [128, G * 3], F32))
        xp = E(nc.sbuf_tensor("xp", [128, TC + 3], F32))
        xptail = E(nc.sbuf_tensor("xptail", [128, G * 3], F32))
        cacc = E(nc.sbuf_tensor("cacc", [128, TC], F32))
        btr = E(nc.sbuf_tensor("btr", [16, T_OWN], BF16))
        ctrt = E(nc.sbuf_tensor("ctrt", [16, T_OWN], BF16))

        states = E(nc.sbuf_tensor("states", [128, G * S], F32))
        attnfm = E(nc.sbuf_tensor("attnfm", [128, TC], F32))
        r1tok = E(nc.sbuf_tensor("r1tok", [128, D], F32))
        xhalo = r1tok[0:3, :]
        flag = E(nc.sbuf_tensor("flag", [1, 2], I32))

        x1tok = xtok
        x1b = ar3_t[:, 2048:3072]
        otok = r1tok
        ffnfm = attnfm
        zab = ar3_t[0:16, 6144:6144 + TC]

        ps_mm = E(nc.psum_tensor("ps_mm", [128, 512], F32))
        ps_mm2 = E(nc.psum_tensor("ps_mm2", [128, 512], F32))
        ps_tr = E(nc.psum_tensor("ps_tr", [128, 512], F32))
        ps_y = E(nc.psum_tensor("ps_y", [128, 512], F32))
        ps_bc = E(nc.psum_tensor("ps_bc", [128, 512], F32))
        ps_bt = E(nc.psum_tensor("ps_bt", [16, 512], F32))
        ps_ct = E(nc.psum_tensor("ps_ct", [16, 512], F32))
        ps_za = E(nc.psum_tensor("ps_za", [16, 512], F32))

        sems = {}
        for name in ("ld", "ls", "lh", "lx", "lc", "lr", "lw0", "lw1",
                     "lp0", "lp1", "sw0", "sw1", "sx", "sd",
                     "gd", "te", "ve", "se", "gp", "st", "fl"):
            sems[name] = E(nc.semaphore("s_" + name))
        cnt = {k: 0 for k in sems}
        block = E(nc.Block())

        SY, VE, SE, TE, GP = "sync", "vector", "scalar", "tensor", "gpsimd"
        streams = {k: [] for k in (SY, VE, SE, TE, GP)}
        waited = {k: {s: 0 for s in sems} for k in streams}
        guards = {}

        def em(engine, fn):
            streams[engine].append(fn)

        def wait(engine, semname, val):
            if val <= 0:
                return
            if waited[engine][semname] >= val:
                return
            waited[engine][semname] = val
            h = sems[semname]
            em(engine, lambda e: e.wait_ge(h, val))

        def bump(semname, k=1):
            cnt[semname] += k
            return cnt[semname]

        def dma(engine, semname, out_ap, in_ap, **kw):
            v = bump(semname, 16)
            h = sems[semname]
            em(engine, lambda e: e.dma_start(out=out_ap, in_=in_ap, **kw)
               .then_inc(h, 16))
            return v

        def op(engine, semname, fn):
            v = bump(semname, 1)
            h = sems[semname]
            em(engine, lambda e: fn(e).then_inc(h, 1))
            return v

        def opq(engine, fn):
            em(engine, fn)

        def guard(engine, res):
            g = guards.get(res)
            if g:
                wait(engine, g[0], g[1])

        def consumed(res, semname, val):
            guards[res] = (semname, val)

        ESEM = {VE: "ve", SE: "se", TE: "te", GP: "gp", SY: "st"}

        dma(SY, "ld", prms[:], prm)
        dma(SY, "ld", mask[:], maskc)
        dma(SY, "ld", idb[:], identb)
        dma(SY, "ld", idf[:], identf)
        dma(SY, "ld", ones1[:], onesb)
        for j in range(DCH):
            dma(SY, "ld", lngc[:, j:j + 1], ln_g[ts(j, 128)].unsqueeze(1))
            dma(SY, "ld", lnbc[:, j:j + 1], ln_b[ts(j, 128)].unsqueeze(1))
            dma(SY, "ld", boutc[:, j:j + 1], b_out[ts(j, 128)].unsqueeze(1))
            dma(SY, "ld", bpjc[:, j:j + 1], b_pj[ts(j, 128)].unsqueeze(1))
        for j in range(NCH_P):
            dma(SY, "ld", binc[:, j:j + 1], b_in[ts(j, 128)].unsqueeze(1))
        for j in range(FCH):
            dma(SY, "ld", bfcc[:, j:j + 1], b_fc[ts(j, 128)].unsqueeze(1))
        for g in range(G):
            dma(SY, "ld", amat[:, ts(g, S)], amat_in[ts(g, 128), :])
            dma(SY, "ld", cwc[:, ts(g, 4)], conv_w[ts(g, 128), :])
            dma(SY, "ld", cbc[:, g:g + 1], conv_b[ts(g, 128)].unsqueeze(1))
            dma(SY, "ld", dcol[:, g:g + 1], dvec[ts(g, 128)].unsqueeze(1))
        dma(SY, "ld", selb[:], selb_in)
        dma(SY, "ld", onesf[:], onesf_in)
        dma(SY, "ld", bBc[:], b_bm.unsqueeze(1))
        v_consts = dma(SY, "ld", bCc[:], b_cm.unsqueeze(1))

        wait(SE, "ld", v_consts)
        op(SE, "se", lambda e: e.activation(
            amat[:], amat[:], ACT.Sigmoid, bias=0.0, scale=1.0))
        wait(VE, "ld", v_consts)
        opq(VE, lambda e: e.memset(epsc[:], EPS))
        op(VE, "ve", lambda e: e.memzero(states[:]))
        wait(TE, "ld", v_consts)

        def quant_weight(src, rows, cols, dst, scol, r16col):
            nsl = rows // 128
            for sl in range(nsl):
                guard(SY, "wslab")
                v = dma(SY, "ls", wslab[:, 0:cols], src[ts(sl, 128), :])
                wait(VE, "ls", v)
                v_ve = op(VE, "ve", lambda e, cols=cols, sl=sl: e.reduce_max(
                    redc[:, sl + 2:sl + 3], wslab[:, 0:cols], axis=AX.X,
                    apply_absolute_value=True))
                consumed("wslab", "ve", v_ve)
            opq(VE, lambda e: e.memset(bnst[:, 0:4], 0.0))
            opq(VE, lambda e: e.memset(bnst[:, 0:4], 0.0))
            opq(VE, lambda e, nsl=nsl: e.reduce_max(
                redc[:, 0:1], redc[:, 2:2 + nsl], axis=AX.X))
            opq(VE, lambda e: e.memset(bnst[:, 0:4], 0.0))
            opq(VE, lambda e: e.memset(bnst[:, 0:4], 0.0))
            guard(VE, "wslab")
            v_ve = op(VE, "ve", lambda e: e.tensor_max(
                wslab[:, 0:128], redc[:, 0:1].broadcast_to([128, 128]),
                redc[:, 0:1].broadcast_to([128, 128])))
            wait(TE, "ve", v_ve)
            guard(TE, "ps_tr")
            v_te = op(TE, "te", lambda e: e.transpose(
                ps_tr[:, 0:128], wslab[:, 0:128], idf[:]))
            consumed("wslab", "te", v_te)
            wait(VE, "te", v_te)
            opq(VE, lambda e: e.reduce_max(
                red1[0:1, 0:1], ps_tr[0:1, 0:128], axis=AX.X))
            opq(VE, lambda e: e.memset(bnst[:, 0:4], 0.0))
            opq(VE, lambda e: e.memset(bnst[:, 0:4], 0.0))
            v_ve = op(VE, "ve", lambda e: e.tensor_scalar(
                red1[0:1, 1:2], red1[0:1, 0:1], 1.0 / 127.0, None, AL.mult))
            consumed("ps_tr", "ve", v_ve)
            opq(VE, lambda e: e.memset(bnst[:, 0:4], 0.0))
            opq(VE, lambda e: e.memset(bnst[:, 0:4], 0.0))
            opq(VE, lambda e: e.reciprocal(
                red1[0:1, 0:1], red1[0:1, 1:2]))
            opq(VE, lambda e: e.memset(bnst[:, 0:4], 0.0))
            v_s = op(VE, "ve", lambda e: e.memset(bnst[:, 0:4], 0.0))
            wait(TE, "ve", v_s)
            guard(TE, "ps_bc")
            v_te = op(TE, "te", lambda e: e.matmul(
                ps_bc[:, 4:6], onesf[0:1, :], red1[0:1, 0:2],
                start=True, stop=True, skip_group_check=True))
            wait(VE, "te", v_te)
            opq(VE, lambda e, scol=scol: e.tensor_copy(
                scl[:, scol:scol + 2], ps_bc[:, 4:6]))
            opq(VE, lambda e, r16col=r16col: e.tensor_scalar(
                scl16[:, r16col:r16col + 1], ps_bc[0:16, 4:5],
                2.0, None, AL.mult))
            opq(VE, lambda e: e.memset(bnst[:, 0:4], 0.0))
            v_gp = op(VE, "ve", lambda e: e.memset(bnst[:, 0:4], 0.0))
            consumed("ps_bc", "ve", v_gp)
            nblk = cols // 128
            half = 1
            for sl in range(nsl):
                guard(SY, "wslab")
                v = dma(SY, "ls", wslab[:, 0:cols], src[ts(sl, 128), :])
                wait(VE, "ls", v)
                for ch in range(cols // 512):
                    cs = ch * 512
                    guard(VE, "kfq")
                    opq(VE, lambda e, cs=cs, scol=scol: e.tensor_scalar(
                        ki32[:], wslab[:, cs:cs + 512],
                        scl[:, scol:scol + 1], None, AL.mult))
                    opq(VE, lambda e: e.tensor_scalar(
                        kfq[:], ki32[:], -1.0, None, AL.add))
                    opq(VE, lambda e, cs=cs, scol=scol:
                        e.scalar_tensor_tensor(
                            wslab[:, cs:cs + 512], wslab[:, cs:cs + 512],
                            scl[:, scol:scol + 1], kfq[:],
                            AL.mult, AL.subtract))
                    opq(VE, lambda e, cs=cs: e.scalar_tensor_tensor(
                        kfq[:], wslab[:, cs:cs + 512], 1.5, kfq[:],
                        AL.is_gt, AL.add))
                    v_q = op(VE, "ve", lambda e, cs=cs: e.scalar_tensor_tensor(
                        kfq[:], wslab[:, cs:cs + 512], 0.5, kfq[:],
                        AL.is_ge, AL.add))
                    consumed("wslab", "ve", v_q)
                    wait(TE, "ve", v_q)
                    guard(TE, "ps_tr")
                    for k in range(4):
                        v_te = op(TE, "te", lambda e, k=k: e.transpose(
                            ps_tr[:, ts(k, 128)], kfq[:, ts(k, 128)],
                            idf[:]))
                    last_te = v_te
                    wait(SE, "te", v_te)
                    half = (half + 1) % 2
                    guard(SE, "wstg%d" % half)
                    v_se = op(SE, "se", lambda e, half=half: e.activation(
                        wstg[:, half * 512:(half + 1) * 512],
                        ps_tr[:, 0:512], ACT.Copy, bias=0.0, scale=1.0))
                    consumed("ps_tr", "se", v_se)
                    wait(SY, "se", v_se)
                    blk0 = sl * nblk + ch * 4
                    v_st = dma(SY, "sw%d" % half,
                               dst[blk0 * 128:(blk0 + 4) * 128, :]
                               .rearrange("(cb p) f -> p cb f", p=128),
                               wstg[:, half * 512:(half + 1) * 512]
                               .rearrange("p (cb f) -> p cb f", cb=4))
                    consumed("wstg%d" % half, "sw%d" % half, v_st)
                    consumed("kfq", "te", last_te)

        quant_weight(w_fc, FF, D, wqfc_dram, 0, 0)
        quant_weight(w_pj, D, FF, wqpj_dram, 2, 1)
        v_sw0_final = cnt["sw0"]
        v_sw1_final = cnt["sw1"]
        wait(SY, "ve", cnt["ve"])
        dma(SY, "st", dbg_dram[:, 0:4], scl[:, 0:4])
        dma(SY, "st", dbg_dram[0:1, 4:6], red1[0:1, 0:2])
        v_quant_stores = cnt["st"]

        for dch in range(DCH):
            guard(SY, "wslab")
            v = dma(SY, "ls", wslab[:], w_in[ts(dch, 128), :])
            wait(VE, "ls", v)
            v_wt = op(VE, "ve", lambda e, dch=dch: e.tensor_scalar(
                wtb[:, dch * 2 * DI:(dch + 1) * 2 * DI], wslab[:],
                lngc[:, dch:dch + 1], None, AL.mult))
            wait(TE, "ls", v)
            v_te = 0
            for nco in range(NCH_P):
                v_te = op(TE, "te", lambda e, dch=dch, nco=nco: e.matmul(
                    ps_bc[:, nco:nco + 1], wslab[:, ts(nco, 128)],
                    lnbc[:, dch:dch + 1],
                    start=(dch == 0), stop=(dch == DCH - 1),
                    skip_group_check=True))
            consumed("wslab", "te", v_te)
            guards["wslab"] = ("te", v_te)
        wait(VE, "te", cnt["te"])
        opq(VE, lambda e: e.tensor_copy(c2c[:, 0:NCH_P], ps_bc[:, 0:NCH_P]))
        opq(VE, lambda e: e.memset(bnst[:, 0:4], 0.0))
        opq(VE, lambda e: e.memset(bnst[:, 0:4], 0.0))
        opq(VE, lambda e: e.tensor_add(c2c[:], c2c[:], binc[:]))
        opq(VE, lambda e: e.memset(bnst[:, 0:4], 0.0))
        v_c2 = op(VE, "ve", lambda e: e.memset(bnst[:, 0:4], 0.0))
        consumed("ps_bc", "ve", v_c2)
        v_wtb_done = v_c2


        for g in range(G):
            guard(SY, "wslab")
            v = dma(SY, "ls", wslab[:, 0:S], w_bm[ts(g, 128), :])
            wait(VE, "ls", v)
            v_ve = op(VE, "ve", lambda e, g=g: e.tensor_copy(
                wB[:, ts(g, S)], wslab[:, 0:S]))
            consumed("wslab", "ve", v_ve)
            guard(SY, "wslab")
            v = dma(SY, "ls", wslab[:, 0:S], w_cm[ts(g, 128), :])
            wait(VE, "ls", v)
            v_ve = op(VE, "ve", lambda e, g=g: e.tensor_copy(
                wC[:, ts(g, S)], wslab[:, 0:S]))
            consumed("wslab", "ve", v_ve)

        def small_transpose(load_ap, nrow, ncolblk, store_fn):
            guard(SY, "wslab")
            v = dma(SY, "ls", wslab[0:nrow, 0:ncolblk * 128], load_ap)
            wait(TE, "ls", v)
            last = 0
            for cb in range(ncolblk):
                guard(TE, "ps_tr0")
                v_te = op(TE, "te", lambda e, cb=cb, nrow=nrow: e.transpose(
                    ps_tr[:, 0:nrow], wslab[0:nrow, ts(cb, 128)],
                    idf[0:nrow, 0:nrow]))
                wait(VE, "te", v_te)
                v_ve = store_fn(cb)
                consumed("ps_tr0", "ve", v_ve)
                last = v_te
            consumed("wslab", "te", last)

        small_transpose(a_fc, 16, DCH, lambda cb: op(
            VE, "ve", lambda e, cb=cb: e.tensor_copy(
                afcT[:, ts(cb, 16)], ps_tr[:, 0:16])))
        small_transpose(a_pj[:, 0:2048], 16, 16, lambda cb: op(
            VE, "ve", lambda e, cb=cb: e.tensor_copy(
                apjT[:, ts(cb, 16)], ps_tr[:, 0:16])))
        small_transpose(a_pj[:, 2048:4096], 16, 16, lambda cb: op(
            VE, "ve", lambda e, cb=cb: e.tensor_copy(
                apjT[:, ts(16 + cb, 16)], ps_tr[:, 0:16])))
        for dch in range(DCH):
            guard(SY, "wslab")
            v = dma(SY, "ls", wslab[:, 0:16], bl_pj[ts(dch, 128), :])
            wait(TE, "ls", v)
            guard(TE, "ps_tr0")
            v_te = op(TE, "te", lambda e: e.transpose(
                ps_tr[0:16, 0:128], wslab[:, 0:16], idf[:]))
            consumed("wslab", "te", v_te)
            wait(VE, "te", v_te)
            v_ve = op(VE, "ve", lambda e, dch=dch: e.tensor_copy(
                bpjT[:, ts(dch, 128)], ps_tr[0:16, 0:128]))
            consumed("ps_tr0", "ve", v_ve)
        v_p1_ve = cnt["ve"]

        def ln_apply(src_ap, nrows, out_ap):
            def settle():
                opq(VE, lambda e: e.memset(bnst[:, 0:4], 0.0))
                opq(VE, lambda e: e.memset(bnst[:, 0:4], 0.0))
            guard(VE, "bnst")
            guard(SE, "cacc")
            v_sq1 = op(SE, "se", lambda e: e.activation(
                cacc[0:nrows, :], src_ap[:, 0:512], ACT.Square,
                bias=0.0, scale=1.0))
            opq(VE, lambda e: e.reduce_sum(
                bnag[0:nrows, 0:1], src_ap, axis=AX.X))
            wait(VE, "se", v_sq1)
            v_rd1 = op(VE, "ve", lambda e: e.reduce_sum(
                bnag[0:nrows, 1:2], cacc[0:nrows, :], axis=AX.X))
            wait(SE, "ve", v_rd1)
            v_sq2 = op(SE, "se", lambda e: e.activation(
                cacc[0:nrows, :], src_ap[:, 512:1024], ACT.Square,
                bias=0.0, scale=1.0))
            wait(VE, "se", v_sq2)
            opq(VE, lambda e: e.reduce_sum(
                rrc[0:nrows, :], cacc[0:nrows, :], axis=AX.X))
            opq(VE, lambda e: e.tensor_scalar(
                bnag[0:nrows, 2:3], bnag[0:nrows, 0:1], 1.0 / D, None,
                AL.mult))
            v_cacc = op(VE, "ve", lambda e: e.memset(bnst[:, 0:4], 0.0))
            consumed("cacc", "ve", v_cacc)
            opq(VE, lambda e: e.tensor_add(
                bnag[0:nrows, 1:2], bnag[0:nrows, 1:2], rrc[0:nrows, :]))
            opq(VE, lambda e: e.scalar_tensor_tensor(
                bnag[0:nrows, 3:4], bnag[0:nrows, 2:3], -1.0,
                bnag[0:nrows, 2:3], AL.mult, AL.mult))
            settle()
            opq(VE, lambda e: e.scalar_tensor_tensor(
                rrc[0:nrows, :], bnag[0:nrows, 1:2], 1.0 / D,
                bnag[0:nrows, 3:4], AL.mult, AL.add))
            settle()
            v_ve = op(VE, "ve", lambda e: e.memset(bnst[:, 0:4], 0.0))
            wait(SE, "ve", v_ve)
            v_se = op(SE, "se", lambda e: e.activation(
                nmr[0:nrows, :], rrc[0:nrows, :], ACT.Sqrt,
                bias=epsc[0:nrows, :], scale=1.0))
            wait(VE, "se", v_se)
            opq(VE, lambda e: e.reciprocal(rrc[0:nrows, :], nmr[0:nrows, :]))
            settle()
            opq(VE, lambda e: e.scalar_tensor_tensor(
                nmr[0:nrows, :], bnag[0:nrows, 2:3], -1.0, rrc[0:nrows, :],
                AL.mult, AL.mult))
            settle()
            v_ve = op(VE, "ve", lambda e: e.memset(bnst[:, 0:4], 0.0))
            consumed("bnst", "ve", v_ve)
            wait(SE, "ve", v_ve)
            return op(SE, "se", lambda e: e.activation(
                out_ap, src_ap, ACT.Identity, bias=nmr[0:nrows, :],
                scale=rrc[0:nrows, :]))

        wait(SE, "ld", v_consts)
        wait(VE, "ld", v_consts)
        wait(SE, "sw0", v_sw0_final)
        wait(SE, "sw1", v_sw1_final)

        for tc in range(NTC):
            if tc == 0:
                v = dma(SY, "lh", xhalo[:], xin[0:3, :])
                wait(VE, "lh", v)
                wait(SE, "lh", v)
                v_xnh = ln_apply(xhalo[:, :], 3, xhalo[:, :])
                wait(TE, "se", v_xnh)
                guard(TE, "ps_tr0")
                v_te = 0
                for dch in range(DCH):
                    v_te = op(TE, "te", lambda e, dch=dch: e.transpose(
                        ps_tr[:, dch * 4: dch * 4 + 3],
                        xhalo[:, ts(dch, 128)], idf[0:3, 0:3]))
                consumed("r1tok", "te", v_te)
                wait(SE, "te", v_te)
                v_se = 0
                for dch in range(DCH):
                    v_se = op(SE, "se", lambda e, dch=dch: e.activation(
                        xnhfm[:, dch * 3: dch * 3 + 3],
                        ps_tr[:, dch * 4: dch * 4 + 3], ACT.Copy,
                        bias=0.0, scale=1.0))
                consumed("ps_tr0", "se", v_se)
                wait(TE, "se", v_se)
                wait(TE, "ve", v_wtb_done)
                guard(TE, "ps_mm")
                v_te = 0
                for g in range(G):
                    for dch in range(DCH):
                        v_te = op(TE, "te", lambda e, g=g, dch=dch: e.matmul(
                            ps_mm[:, g * 3:(g + 1) * 3],
                            wtb[:, dch * 2 * DI + g * 128:
                                dch * 2 * DI + (g + 1) * 128],
                            xnhfm[:, ts(dch, 3)],
                            start=(dch == 0), stop=(dch == DCH - 1),
                            skip_group_check=True))
                wait(VE, "te", v_te)
                v_tail = 0
                for g in range(G):
                    opq(VE, lambda e, g=g: e.tensor_scalar(
                        xhsc[:, ts(g, 3)], ps_mm[:, ts(g, 3)],
                        c2c[:, g:g + 1], None, AL.add))
                for g in range(G):
                    opq(VE, lambda e, g=g: e.tensor_scalar(
                        xptail[:, ts(g, 3)], xhsc[:, ts(g, 3)],
                        mask[:], None, AL.mult))
                opq(VE, lambda e: e.memset(bnst[:, 0:4], 0.0))
                v_tail = op(VE, "ve", lambda e: e.memset(bnst[:, 0:4], 0.0))
                consumed("ps_mm", "ve", v_tail)

            for tb in range(4):
                trow = tc * TC + tb * 128
                guard(SY, "xtok")
                v = dma(SY, "lx", xtok[:], xin[3 + trow: 3 + trow + 128, :])
                wait(VE, "lx", v)
                wait(SE, "lx", v)
                v_xn = ln_apply(xtok[:, :], 128, xtok[:, :])
                wait(TE, "se", v_xn)
                last_tr = 0
                for g4 in range(2):
                    guard(TE, "ps_tr")
                    for k in range(4):
                        dch = g4 * 4 + k
                        v_te = op(TE, "te", lambda e, dch=dch, k=k: e.transpose(
                            ps_tr[:, ts(k, 128)], xtok[:, ts(dch, 128)], idf[:]))
                    last_tr = v_te
                    wait(SE, "te", v_te)
                    v_se = op(SE, "se", lambda e, g4=g4, tb=tb: e.activation(
                        xnfm.rearrange("p (dch t) -> p dch t", dch=DCH)
                        [:, g4 * 4:(g4 + 1) * 4, tb * 128: tb * 128 + 128],
                        ps_tr[:, 0:512].rearrange("p (k t) -> p k t", k=4),
                        ACT.Copy, bias=0.0, scale=1.0))
                    consumed("ps_tr", "se", v_se)
                consumed("xtok", "te", last_tr)
            v_xn_all = cnt["se"]

            wait(TE, "se", v_xn_all)
            wait(TE, "ve", v_wtb_done)
            for nch in range(NCH_P):
                psname = "ps_mm" if nch % 2 == 0 else "ps_mm2"
                ps = ps_mm if nch % 2 == 0 else ps_mm2
                guard(TE, psname)
                for dch in range(DCH):
                    op(TE, "te", lambda e, nch=nch, dch=dch, ps=ps: e.matmul(
                        ps[:, 0:TC],
                        wtb[:, dch * 2 * DI + nch * 128:
                            dch * 2 * DI + (nch + 1) * 128],
                        xnfm[:, ts(dch, TC)],
                        start=(dch == 0), stop=(dch == DCH - 1),
                        skip_group_check=True))
                v_mm = cnt["te"]
                if nch < G:
                    g = nch
                    wait(SE, "te", v_mm)
                    guard(SE, "xp")
                    v_se = op(SE, "se", lambda e, g=g, ps=ps: e.activation(
                        xp[:, 3:3 + TC], ps[:, 0:TC], ACT.Identity,
                        bias=c2c[:, g:g + 1], scale=1.0))
                    consumed(psname, "se", v_se)
                    wait(VE, "se", v_se)
                    if tc == 0:
                        wait(VE, "ve", v_tail)
                    opq(VE, lambda e, g=g: e.tensor_copy(
                        xp[:, 0:3], xptail[:, ts(g, 3)]))
                    guard(VE, "cacc")
                    opq(VE, lambda e: e.memset(bnst[:, 0:4], 0.0))
                    opq(VE, lambda e: e.memset(bnst[:, 0:4], 0.0))
                    opq(VE, lambda e, g=g: e.tensor_scalar(
                        cacc[:], xp[:, 0:TC], cwc[:, g * 4:g * 4 + 1],
                        None, AL.mult))
                    for k in (1, 2, 3):
                        opq(VE, lambda e, g=g, k=k: e.scalar_tensor_tensor(
                            cacc[:], xp[:, k:k + TC],
                            cwc[:, g * 4 + k:g * 4 + k + 1],
                            cacc[:], AL.mult, AL.add))
                    v_ve = op(VE, "ve", lambda e, g=g: e.tensor_copy(
                        xptail[:, ts(g, 3)], xp[:, TC:TC + 3]))
                    consumed("xp", "ve", v_ve)
                    wait(SE, "ve", v_ve)
                    guard(SE, "xcch_%d" % g)
                    v_xc = op(SE, "se", lambda e, g=g: e.activation(
                        xcch[:, ts(g, TC)], cacc[:], ACT.Silu,
                        bias=cbc[:, g:g + 1], scale=1.0))
                    consumed("cacc", "se", v_xc)
                    wait(TE, "se", v_xc)
                    if g == 0:
                        guard(TE, "ps_bt")
                        guard(TE, "ps_ct")
                    op(TE, "te", lambda e, g=g: e.matmul(
                        ps_bt[:, 0:TC], wB[:, ts(g, S)], xcch[:, ts(g, TC)],
                        start=(g == 0), stop=(g == G - 1),
                        skip_group_check=True))
                    v_te2 = op(TE, "te", lambda e, g=g: e.matmul(
                        ps_ct[:, 0:TC], wC[:, ts(g, S)], xcch[:, ts(g, TC)],
                        start=(g == 0), stop=(g == G - 1),
                        skip_group_check=True))
                    consumed("xcch_%d" % g, "te", v_te2)
                else:
                    g = nch - G
                    wait(SE, "te", v_mm)
                    guard(SE, "sgch_%d" % g)
                    v_se = op(SE, "se", lambda e, g=g, nch=nch, ps=ps:
                              e.activation(
                                  sgch[:, ts(g, TC)], ps[:, 0:TC], ACT.Silu,
                                  bias=c2c[:, nch:nch + 1], scale=1.0))
                    consumed(psname, "se", v_se)
                    consumed("sg_se_%d" % g, "se", v_se)
            v_bt_mm = cnt["te"]
            wait(SE, "te", v_bt_mm)
            v_bt = op(SE, "se", lambda e, tc=tc: e.activation(
                btr[:, ts(tc, TC)], ps_bt[:, 0:TC], ACT.Identity,
                bias=bBc[:], scale=1.0))
            consumed("ps_bt", "se", v_bt)
            v_ct = op(SE, "se", lambda e, tc=tc: e.activation(
                ctrt[:, ts(tc, TC)], ps_ct[:, 0:TC], ACT.Identity,
                bias=bCc[:], scale=1.0))
            consumed("ps_ct", "se", v_ct)
            for g in range(G):
                wait(SY, "se", guards.get("sg_se_%d" % g, ("se", 0))[1])
                dma(SY, "sx", xc_dram[ts(g, 128), ts(tc, TC)],
                    xcch[:, ts(g, TC)])
                dma(SY, "sx", sg_dram[ts(g, 128), ts(tc, TC)],
                    sgch[:, ts(g, TC)])
            v_burst = cnt["sx"]
            for g in range(G):
                consumed("xcch_%d" % g, "sx", v_burst)
                consumed("sgch_%d" % g, "sx", v_burst)
        v_a1_stores = cnt["sx"]
        v_a1_se = cnt["se"]

        wait(GP, "ld", v_consts)
        gd_base = cnt["gd"]
        fl_base = cnt["fl"]
        shf_h, shs_h = shf, shs
        sem_fl_h, sem_gd_h = sems["fl"], sems["gd"]

        def gp_handshake_recv(e):
            is_first = e.value_load(prms[0:1, 0:1])
            partrow = e.value_load(prms[0:1, 2:3])
            partflag = e.value_load(prms[0:1, 4:5])
            with e.If(is_first == 0):
                with e.register("cnt") as creg:
                    e.reg_mov(creg, fl_base)

                    def cond():
                        e.reg_add(creg, creg, 16)
                        e.dma_start(out=flag[:],
                                    in_=shf_h[bass.ds(partflag, 1), :]) \
                            .then_inc(sem_fl_h, 16)
                        e.wait_ge(sem_fl_h, creg)
                        v0 = e.value_load(flag[0:1, 0:1])
                        v1 = e.value_load(flag[0:1, 1:2])
                        return (v0 != M0) | (v1 != M1)
                    with e.While(cond):
                        pass
                e.dma_start(out=states[:].bitcast(I32),
                            in_=shs_h[bass.ds(partrow, 128), :].bitcast(I32)) \
                    .then_inc(sem_gd_h, 16)
                e.wait_ge(sem_gd_h, gd_base + 16)
            with e.Else():
                e.dma_start(out=flag[:], in_=shf_h[0:1, :]) \
                    .then_inc(sem_gd_h, 16)
                e.wait_ge(sem_gd_h, gd_base + 16)

        em(GP, gp_handshake_recv)
        bump("gd", 16)
        v_hs_recv = op(GP, "gp", lambda e: e.engine_nop())
        gp_scan_base = cnt["gp"]

        wait(VE, "gp", v_hs_recv)
        wait(SY, "sx", v_a1_stores)
        wait(TE, "se", v_a1_se)
        wait(TE, "ve", v_p1_ve)
        wait(GP, "te", cnt["te"])
        v_wo_fill = 0
        for g_ in range(G):
            v_wo_fill = dma(GP, "gd", woutb[:, g_ * D:(g_ + 1) * D],
                            w_out[ts(g_, 128), :])
        wait(TE, "gd", v_wo_fill)

        for tc in range(NTC):
            for g in range(G):
                wait(SY, "ve", guards.get("a2_xcch_%d" % g, ("ve", 0))[1])
                dma(SY, "lc", xcch[:, ts(g, TC)],
                    xc_dram[ts(g, 128), ts(tc, TC)])
                wait(SY, "ve", guards.get("a2_sgch_%d" % g, ("ve", 0))[1])
                dma(SY, "lc", sgch[:, ts(g, TC)],
                    sg_dram[ts(g, 128), ts(tc, TC)])
            v_ld_chunks = cnt["lc"]
            wait(SE, "ve", guards.get("bc128", ("ve", 0))[1])
            for s in range(S):
                guard(TE, "ps_bc")
                v_te = op(TE, "te", lambda e, s=s, tc=tc: e.matmul(
                    ps_bc[:, 0:TC], selb[:, ts(s, 128)], btr[0:16, ts(tc, TC)],
                    start=True, stop=True, skip_group_check=True))
                wait(SE, "te", v_te)
                v_se = op(SE, "se", lambda e, s=s: e.activation(
                    b128[:, ts(s, TC)], ps_bc[:, 0:TC], ACT.Copy,
                    bias=0.0, scale=1.0))
                consumed("ps_bc", "se", v_se)
                guard(TE, "ps_bc")
                v_te = op(TE, "te", lambda e, s=s, tc=tc: e.matmul(
                    ps_bc[:, 0:TC], selb[:, ts(s, 128)], ctrt[0:16, ts(tc, TC)],
                    start=True, stop=True, skip_group_check=True))
                wait(SE, "te", v_te)
                v_se = op(SE, "se", lambda e, s=s: e.activation(
                    c128[:, ts(s, TC)], ps_bc[:, 0:TC], ACT.Copy,
                    bias=0.0, scale=1.0))
                consumed("ps_bc", "se", v_se)
            v_bc = cnt["se"]
            wait(VE, "se", v_bc)
            wait(VE, "lc", v_ld_chunks)

            for g in range(G):
                wait(VE, "gp", gp_scan_base + tc * G + g)
                guard(VE, "ubuf")
                opq(VE, lambda e, g=g: e.tensor_tensor(
                    ubuf[:].rearrange("p (s t) -> p s t", s=S),
                    xcch[:, ts(g, TC)].unsqueeze(1).broadcast_to([128, S, TC]),
                    b128[:].rearrange("p (s t) -> p s t", s=S),
                    AL.mult))
                for s in range(S):
                    opq(VE, lambda e, g=g, s=s: e.tensor_tensor_scan(
                        hbuf[:, ts(s, TC)],
                        amat[:, g * S + s: g * S + s + 1]
                            .broadcast_to([128, TC]),
                        ubuf[:, ts(s, TC)],
                        states[:, g * S + s: g * S + s + 1],
                        AL.mult, AL.add))
                v_scan = op(VE, "ve", lambda e: e.engine_nop())
                wait(GP, "ve", v_scan)
                guard(GP, "std")
                opq(GP, lambda e, g=g: e.tensor_copy(
                    states[:, g * S:(g + 1) * S].unsqueeze(2),
                    hbuf[:].rearrange("p (s t) -> p s t", s=S)
                    [:, :, TC - 1:TC]))
                opq(GP, lambda e: e.memset(redc[:, 0:32], 0.0))
                opq(GP, lambda e: e.memset(redc[:, 0:32], 0.0))
                opq(GP, lambda e: e.memset(redc[:, 0:32], 0.0))
                op(GP, "gp", lambda e: e.engine_nop())
                opq(VE, lambda e: e.tensor_tensor(
                    ubuf[:], hbuf[:], c128[:], AL.mult))
                v_ch = op(VE, "ve", lambda e: e.engine_nop())
                wait(TE, "ve", v_ch)
                guard(TE, "ps_y")
                last = 0
                for s in range(S):
                    last = op(TE, "te", lambda e, s=s: e.matmul(
                        ps_y[:, 0:TC], idb[:], ubuf[:, ts(s, TC)],
                        start=(s == 0), stop=(s == S - 1),
                        skip_group_check=True))
                consumed("ubuf", "te", last)
                wait(VE, "te", last)
                guard(VE, "cacc")
                opq(VE, lambda e, g=g: e.scalar_tensor_tensor(
                    cacc[:], xcch[:, ts(g, TC)], dcol[:, g:g + 1],
                    ps_y[:, 0:TC], AL.mult, AL.add))
                guard(VE, "ybd")
                v_y = op(VE, "ve", lambda e, g=g: e.tensor_tensor(
                    ybuf[:, ts(g, TC)], cacc[:], sgch[:, ts(g, TC)], AL.mult))
                consumed("ps_y", "ve", v_y)
                consumed("cacc", "ve", v_y)
                consumed("a2_xcch_%d" % g, "ve", v_y)
                consumed("a2_sgch_%d" % g, "ve", v_y)
                consumed("ybuf_w_%d" % g, "ve", v_y)
            v_y_all = cnt["ve"]
            consumed("bc128", "ve", v_y_all)
            if _dbg:
                wait(SY, "ve", v_y_all)
                vdmp = dma(SY, "sd", yb_dram[tc * 128:(tc + 1) * 128, :],
                           ybuf[:, 0:G * TC])
                consumed("ybd", "sd", vdmp)
                wait(SY, "gp", gp_scan_base + (tc + 1) * G)
                vdmp = dma(SY, "sd", st_dram[tc * 128:(tc + 1) * 128, :],
                           states[:])
                consumed("std", "sd", vdmp)

            wait(TE, "ve", v_y_all)
            for dch in range(DCH):
                guard(TE, "ps_mm")
                for g in range(G):
                    op(TE, "te", lambda e, dch=dch, g=g: e.matmul(
                        ps_mm[:, 0:TC],
                        woutb[:, g * D + dch * 128: g * D + (dch + 1) * 128],
                        ybuf[:, ts(g, TC)],
                        start=(g == 0), stop=(g == G - 1),
                        skip_group_check=True))
                v_wo = cnt["te"]
                wait(SE, "te", v_wo)
                guard(SE, "attnfm")
                guard(SE, "attnfm_d")
                v_se = op(SE, "se", lambda e, dch=dch: e.activation(
                    attnfm[:, 0:TC], ps_mm[:, 0:TC], ACT.Identity,
                    bias=boutc[:, dch:dch + 1], scale=1.0))
                consumed("ps_mm", "se", v_se)
                wait(TE, "se", v_se)
                guard(TE, "ps_tr")
                for tb in range(4):
                    v_tr = op(TE, "te", lambda e, tb=tb: e.transpose(
                        ps_tr[:, ts(tb, 128)], attnfm[:, ts(tb, 128)], idf[:]))
                last_tr = v_tr
                wait(VE, "te", v_tr)
                guard(VE, "xa_d")
                v_ve = op(VE, "ve", lambda e, dch=dch: e.tensor_copy(
                    x1fs.rearrange("p (tb d) -> p tb d", tb=4)
                    [:, :, dch * 128:(dch + 1) * 128],
                    ps_tr[:, 0:512].rearrange("p (k t) -> p k t", k=4)))
                consumed("ps_tr", "ve", v_ve)
                consumed("attnfm", "te", last_tr)
                if _dbg:
                    wait(SY, "se", v_se)
                    vdmp = dma(SY, "sd",
                               at_dram[(tc * DCH + dch) * 128:
                                       (tc * DCH + dch + 1) * 128, :],
                               attnfm[:, 0:TC])
                    consumed("attnfm_d", "sd", vdmp)
            v_r1p = cnt["ve"]
            if _dbg:
                wait(SY, "ve", v_r1p)
                vdmp = dma(SY, "sd", xa_dram[tc * 128:(tc + 1) * 128, :],
                           x1fs[:, 0:4 * D])
                consumed("xa_d", "sd", vdmp)

            for tb in range(4):
                trow = tc * TC + tb * 128
                guard(SY, "xtok")
                v = dma(SY, "lx", xtok[:], xin[3 + trow: 3 + trow + 128, :])
                wait(VE, "lx", v)
                wait(VE, "ve", v_r1p)
                guard(VE, "r1tok")
                v_r1 = op(VE, "ve", lambda e, tb=tb: e.scalar_tensor_tensor(
                    r1tok[:], xtok[:], 2.0, x1fs[:, tb * D:(tb + 1) * D],
                    AL.mult, AL.add))
                consumed("xtok", "ve", v_r1)
                consumed("x1fs_%d" % tb, "ve", v_r1)
                wait(SY, "ve", v_r1)
                v_st = dma(SY, "st", r1_dram[trow: trow + 128, :], r1tok[:])
                consumed("r1tok", "st", v_st)
        v_r1_stores = cnt["st"]

        gd_base2 = cnt["gd"]

        def gp_handshake_send(e):
            e.memset(redc[:, 0:32], 0.0)
            e.memset(redc[:, 0:32], 0.0)
            e.memset(redc[:, 0:32], 0.0)
            is_first = e.value_load(prms[0:1, 0:1])
            myrow = e.value_load(prms[0:1, 1:2])
            myflag = e.value_load(prms[0:1, 3:4])
            with e.If(is_first):
                e.dma_start(out=shs_h[bass.ds(myrow, 128), :], in_=states[:]) \
                    .then_inc(sem_gd_h, 16)
                e.wait_ge(sem_gd_h, gd_base2 + 16)
                e.memset(flag[0:1, 0:1], M0)
                e.memset(flag[0:1, 1:2], M1)
                e.dma_start(out=shf_h[bass.ds(myflag, 1), :], in_=flag[:]) \
                    .then_inc(sem_gd_h, 16)
                e.wait_ge(sem_gd_h, gd_base2 + 32)
            with e.Else():
                e.dma_start(out=flag[:], in_=shf_h[0:1, :]) \
                    .then_inc(sem_gd_h, 16)
                e.dma_start(out=flag[:], in_=shf_h[0:1, :]) \
                    .then_inc(sem_gd_h, 16)
                e.wait_ge(sem_gd_h, gd_base2 + 32)

        em(GP, gp_handshake_send)
        bump("gd", 32)

        wait(SY, "sw0", v_sw0_final)
        wait(SY, "sw1", v_sw1_final)
        wait(SY, "te", cnt["te"])
        wait(SY, "ve", cnt["ve"])
        wait(SY, "se", cnt["se"])
        wait(SY, "st", v_r1_stores)
        for fch in range(FCH):
            guard(SY, "wslab")
            v = dma(SY, "ls", wslab[:, 0:16], bl_fc[ts(fch, 128), :])
            wait(TE, "ls", v)
            guard(TE, "ps_tr0")
            v_te = op(TE, "te", lambda e: e.transpose(
                ps_tr[0:16, 0:128], wslab[:, 0:16], idf[:]))
            consumed("wslab", "te", v_te)
            wait(VE, "te", v_te)
            v_ve = op(VE, "ve", lambda e, fch=fch: e.tensor_copy(
                bfcT[:, ts(fch, 128)], ps_tr[0:16, 0:128]))
            consumed("ps_tr0", "ve", v_ve)


        for tc in range(NTC):
            for tb in range(4):
                trow = tc * TC + tb * 128
                guard(SY, "r1tok")
                v = dma(SY, "lr", r1tok[:], r1_dram[trow: trow + 128, :])
                wait(VE, "lr", v)
                wait(SE, "lr", v)
                guard(SE, "xtok")
                guard(SE, "xtokv")
                v_x1 = ln_apply(r1tok[:, :], 128, x1tok[:, :])
                consumed("r1tok", "se", v_x1)
                if _dbg:
                    wait(SY, "se", v_x1)
                    dma(SY, "sd", x1n_dram[trow: trow + 128, :], x1tok[:])
                wait(VE, "se", v_x1)
                guard(VE, "x1fs_%d" % tb)
                v_cp = op(VE, "ve", lambda e, tb=tb: e.tensor_copy(
                    x1fs[:, tb * D:(tb + 1) * D], x1tok[:]))
                consumed("xtokv", "ve", v_cp)
                wait(TE, "se", v_x1)
                last_tr = 0
                for g4 in range(2):
                    guard(TE, "ps_tr")
                    for k in range(4):
                        dch = g4 * 4 + k
                        v_te = op(TE, "te", lambda e, dch=dch, k=k: e.transpose(
                            ps_tr[:, ts(k, 128)], x1tok[:, ts(dch, 128)], idf[:]))
                    last_tr = v_te
                    wait(SE, "te", v_te)
                    v_se = op(SE, "se", lambda e, g4=g4, tb=tb: e.activation(
                        x1fm.rearrange("p (dch t) -> p dch t", dch=DCH)
                        [:, g4 * 4:(g4 + 1) * 4, tb * 128: tb * 128 + 128],
                        ps_tr[:, 0:512].rearrange("p (k t) -> p k t", k=4),
                        ACT.Copy, bias=0.0, scale=1.0))
                    consumed("ps_tr", "se", v_se)
                consumed("xtok", "te", last_tr)
            v_x1_all = cnt["se"]

            wait(TE, "se", v_x1_all)
            guard(TE, "ps_za")
            for dch in range(DCH):
                op(TE, "te", lambda e, dch=dch: e.matmul(
                    ps_za[:, 0:TC], afcT[:, ts(dch, 16)], x1fm[:, ts(dch, TC)],
                    start=(dch == 0), stop=(dch == DCH - 1),
                    skip_group_check=True))
            v_za = cnt["te"]
            wait(VE, "te", v_za)
            guard(VE, "zab")
            guard(VE, "zabd2")
            v_zab = op(VE, "ve",lambda e: e.tensor_scalar(
                zab[:, 0:TC], ps_za[:, 0:TC], scl16[:, 0:1], None, AL.mult))
            consumed("ps_za", "ve", v_zab)
            if _dbg:
                wait(SY, "ve", v_zab)
                vdmp = dma(SY, "sd", zab_dram[tc * 32: tc * 32 + 16, :],
                           zab[:, 0:TC])
                consumed("zabd1", "sd", vdmp)

            wait(TE, "ve", v_zab)
            for fch in range(FCH):
                slot = fch % 2
                psname = "ps_mm" if fch % 2 == 0 else "ps_mm2"
                ps = ps_mm if fch % 2 == 0 else ps_mm2
                guard(SY, "fcwin%d" % slot)
                vld = dma(SY, "lw%d" % slot,
                          fcwin_t[:, slot * 1024:(slot + 1) * 1024]
                          .rearrange("p (dch f) -> p dch f", dch=DCH),
                          wqfc_dram[fch * 1024:(fch + 1) * 1024, :]
                          .rearrange("(dch p) f -> p dch f", p=128))
                wait(TE, "lw%d" % slot, vld)
                guard(TE, psname)
                for dch in range(DCH):
                    op(TE, "te", lambda e, dch=dch, ps=ps, slot=slot: e.matmul(
                        ps[:, 0:TC],
                        fcwin_t[:, slot * 1024 + dch * 128:
                                slot * 1024 + (dch + 1) * 128],
                        x1fm[:, ts(dch, TC)],
                        start=(dch == 0), stop=False, skip_group_check=True))
                v_lora = op(TE, "te", lambda e, fch=fch, ps=ps: e.matmul(
                    ps[:, 0:TC], bfcT[:, ts(fch, 128)], zab[:, 0:TC],
                    start=False, stop=True, skip_group_check=True))
                consumed("zab", "te", v_lora)
                consumed("fcwin%d" % slot, "te", v_lora)
                wait(SE, "te", v_lora)
                guard(SE, "h1fm_%d" % fch)
                v_se = op(SE, "se", lambda e, fch=fch, ps=ps: e.activation(
                    h1fm[:, ts(fch, TC)], ps[:, 0:TC], ACT.Gelu,
                    bias=bfcc[:, fch:fch + 1], scale=scl[:, 1:2]))
                consumed(psname, "se", v_se)
            v_h1 = cnt["se"]

            wait(TE, "se", v_h1)
            guard(TE, "ps_za")
            for fch in range(FCH):
                op(TE, "te", lambda e, fch=fch: e.matmul(
                    ps_za[:, 0:TC], apjT[:, ts(fch, 16)], h1fm[:, ts(fch, TC)],
                    start=(fch == 0), stop=(fch == FCH - 1),
                    skip_group_check=True))
            v_za2 = cnt["te"]
            wait(VE, "te", v_za2)
            guard(VE, "zab")
            guard(VE, "zabd1")
            v_zab2 = op(VE, "ve", lambda e: e.tensor_scalar(
                zab[:, 0:TC], ps_za[:, 0:TC], scl16[:, 1:2], None, AL.mult))
            consumed("ps_za", "ve", v_zab2)
            if _dbg:
                wait(SY, "ve", v_zab2)
                vdmp = dma(SY, "sd", zab_dram[tc * 32 + 16: tc * 32 + 32, :],
                           zab[:, 0:TC])
                consumed("zabd2", "sd", vdmp)

            wait(TE, "ve", v_zab2)
            for dch in range(DCH):
                guard(TE, "ps_mm")
                for hf in range(4):
                    half = hf % 2
                    guard(SY, "pjw%d" % half)
                    vld = dma(SY, "lp%d" % half,
                              pjwin_t[:, half * 1024: half * 1024 + 1024]
                              .rearrange("p (fch d) -> p fch d", fch=8),
                              wqpj_dram[dch * 4096 + hf * 1024:
                                        dch * 4096 + (hf + 1) * 1024, :]
                              .rearrange("(fch p) d -> p fch d", p=128))
                    wait(TE, "lp%d" % half, vld)
                    for f8 in range(8):
                        fch = hf * 8 + f8
                        v_te = op(TE, "te",
                                  lambda e, half=half, f8=f8, fch=fch:
                                  e.matmul(
                                      ps_mm[:, 0:TC],
                                      pjwin_t[:, half * 1024 + f8 * 128:
                                              half * 1024 + (f8 + 1) * 128],
                                      h1fm[:, ts(fch, TC)],
                                      start=(fch == 0), stop=False,
                                      skip_group_check=True))
                    consumed("pjw%d" % half, "te", v_te)
                v_lora2 = op(TE, "te", lambda e, dch=dch: e.matmul(
                    ps_mm[:, 0:TC], bpjT[:, ts(dch, 128)], zab[:, 0:TC],
                    start=False, stop=True, skip_group_check=True))
                consumed("zab", "te", v_lora2)
                v_pj = cnt["te"]
                wait(SE, "te", v_pj)
                guard(SE, "attnfm")
                v_se = op(SE, "se", lambda e, dch=dch: e.activation(
                    ffnfm[:, 0:TC], ps_mm[:, 0:TC], ACT.Identity,
                    bias=bpjc[:, dch:dch + 1], scale=scl[:, 3:4]))
                consumed("ps_mm", "se", v_se)
                wait(TE, "se", v_se)
                guard(TE, "ps_tr")
                for tb in range(4):
                    v_tr = op(TE, "te", lambda e, tb=tb: e.transpose(
                        ps_tr[:, ts(tb, 128)], ffnfm[:, ts(tb, 128)], idf[:]))
                last_tr = v_tr
                wait(VE, "te", v_tr)
                v_ve = op(VE, "ve", lambda e, dch=dch: e.tensor_tensor(
                    x1fs.rearrange("p (tb d) -> p tb d", tb=4)
                    [:, :, dch * 128:(dch + 1) * 128],
                    x1fs.rearrange("p (tb d) -> p tb d", tb=4)
                    [:, :, dch * 128:(dch + 1) * 128],
                    ps_tr[:, 0:512].rearrange("p (k t) -> p k t", k=4), AL.add))
                consumed("ps_tr", "ve", v_ve)
                consumed("attnfm", "te", last_tr)
            v_r2 = cnt["ve"]

            for tb in range(4):
                trow = tc * TC + tb * 128
                wait(VE, "ve", v_r2)
                wait(SE, "ve", v_r2)
                if _dbg:
                    wait(SY, "ve", v_r2)
                    dma(SY, "sd", x2_dram[trow: trow + 128, :],
                        x1fs[:, tb * D:(tb + 1) * D])
                guard(SE, "r1tok")
                v_o = ln_apply(x1fs[:, tb * D:(tb + 1) * D], 128, otok[:, :])
                consumed("x1fs_%d" % tb, "se", v_o)
                wait(SY, "se", v_o)
                v_st = dma(SY, "st", out[trow: trow + 128, :], otok[:])
                consumed("r1tok", "st", v_st)

        if _dbg:
            dma(SY, "sd", wo_dram[16:128, :], woutb[16:128, 0:G * D])
            dma(SY, "sd", bo_dram, boutc[:])
        v_final = cnt["st"]
        wait(SY, "st", v_final)

        @block.sync
        def _(e):
            for fn in streams[SY]:
                fn(e)

        @block.vector
        def _(e):
            for fn in streams[VE]:
                fn(e)

        @block.scalar
        def _(e):
            for fn in streams[SE]:
                fn(e)

        @block.tensor
        def _(e):
            for fn in streams[TE]:
                fn(e)

        @block.gpsimd
        def _(e):
            for fn in streams[GP]:
                fn(e)

    return nc


_NC_CACHE = None


def kernel(**inputs):
    global _NC_CACHE
    if _NC_CACHE is None:
        _NC_CACHE = build_program()
    nc = _NC_CACHE

    x = np.ascontiguousarray(np.asarray(inputs["L0_attn_x"], np.float32))
    weights = dict(
        w_in=np.ascontiguousarray(np.asarray(inputs["L0_attn_W_in"], np.float32)),
        b_in=np.ascontiguousarray(np.asarray(inputs["L0_attn_b_in"], np.float32)),
        ln_g=np.ascontiguousarray(np.asarray(inputs["L0_attn_ln_g"], np.float32)),
        ln_b=np.ascontiguousarray(np.asarray(inputs["L0_attn_ln_b"], np.float32)),
        conv_w=np.ascontiguousarray(
            np.asarray(inputs["L0_attn_conv_w"], np.float32)[:, 0, :]),
        conv_b=np.ascontiguousarray(np.asarray(inputs["L0_attn_conv_b"], np.float32)),
        amat_in=np.ascontiguousarray(np.asarray(inputs["L0_attn_A"], np.float32)),
        w_bm=np.ascontiguousarray(np.asarray(inputs["L0_attn_W_B"], np.float32)),
        b_bm=np.ascontiguousarray(np.asarray(inputs["L0_attn_b_B"], np.float32)),
        w_cm=np.ascontiguousarray(np.asarray(inputs["L0_attn_W_C"], np.float32)),
        b_cm=np.ascontiguousarray(np.asarray(inputs["L0_attn_b_C"], np.float32)),
        dvec=np.ascontiguousarray(np.asarray(inputs["L0_attn_D"], np.float32)),
        w_out=np.ascontiguousarray(np.asarray(inputs["L0_attn_W_out"], np.float32)),
        b_out=np.ascontiguousarray(np.asarray(inputs["L0_attn_b_out"], np.float32)),
        w_fc=np.ascontiguousarray(np.asarray(inputs["L0_ffn_W_fc"], np.float32)),
        b_fc=np.ascontiguousarray(np.asarray(inputs["L0_ffn_b_fc"], np.float32)),
        a_fc=np.ascontiguousarray(np.asarray(inputs["L0_ffn_A_fc"], np.float32)),
        bl_fc=np.ascontiguousarray(np.asarray(inputs["L0_ffn_B_fc"], np.float32)),
        w_pj=np.ascontiguousarray(np.asarray(inputs["L0_ffn_W_proj"], np.float32)),
        b_pj=np.ascontiguousarray(np.asarray(inputs["L0_ffn_b_proj"], np.float32)),
        a_pj=np.ascontiguousarray(np.asarray(inputs["L0_ffn_A_proj"], np.float32)),
        bl_pj=np.ascontiguousarray(np.asarray(inputs["L0_ffn_B_proj"], np.float32)),
    )
    identb = np.eye(128, dtype=BF)
    identf = np.eye(128, dtype=np.float32)
    onesb = np.ones((1, 128), dtype=BF)
    onesf_h = np.ones((1, 128), dtype=np.float32)
    selb_h = np.zeros((16, 16 * 128), dtype=BF)
    for s_ in range(16):
        selb_h[s_, s_ * 128:(s_ + 1) * 128] = 1

    in_maps = []
    for c in range(NC_CORES):
        b = c // 2
        half = c % 2
        start = half * T_OWN
        if half == 0:
            xin = np.concatenate(
                [np.zeros((3, D), np.float32), x[b, 0:T_OWN]], axis=0)
            maskv = np.zeros((128, 1), np.float32)
        else:
            xin = np.ascontiguousarray(x[b, start - 3: start + T_OWN])
            maskv = np.ones((128, 1), np.float32)
        prm_host = np.array([[1 - half, c * 128, (c ^ 1) * 128,
                              c, c ^ 1, 0, 0, 0]], np.int32)
        m = dict(weights)
        m.update(xin=np.ascontiguousarray(xin), prm=prm_host, maskc=maskv,
                 identb=identb, identf=identf, onesb=onesb, selb_in=selb_h,
                 onesf_in=onesf_h)
        in_maps.append(m)

    res = run_bass_kernel_spmd(nc, in_maps, core_ids=list(range(NC_CORES)))
    out_full = np.empty((B_, T_, D), np.float32)
    for c in range(NC_CORES):
        b = c // 2
        half = c % 2
        out_full[b, half * T_OWN:(half + 1) * T_OWN, :] = res.results[c]["out"]
    return out_full


